# version 06 (final)
# baseline (speedup 1.0000x reference)
import numpy as np
import concourse.bass as bass
import concourse.mybir as mybir
from concourse.bass_utils import run_bass_kernel_spmd

F32 = mybir.dt.float32
F32R = mybir.dt.float32r
BF16 = mybir.dt.bfloat16
I32 = mybir.dt.int32
AF = mybir.ActivationFunctionType
ALU = mybir.AluOpType

B, L, D = 4, 8192, 1024
N = 16384
DH = 512
T = 4096
NSUB = 32
NKSUB = 16
NUSUB = 16


def _consts():
    n2 = np.arange(64)[:, None]
    k2 = np.arange(128)[None, :]
    a = 2 * np.pi * (n2 * k2) / 128.0
    f1 = np.concatenate([np.cos(a), -np.sin(a)], axis=1)
    f1 = np.concatenate([f1, f1], axis=0)

    n1 = np.arange(128)[:, None]
    a = 2 * np.pi * (n1 * k2) / N
    tw = np.concatenate([np.cos(a), -np.sin(a), np.sin(a)], axis=1)

    k1 = np.arange(128)[None, :]
    a = 2 * np.pi * (n1 * k1) / 128.0
    f2 = np.concatenate([np.cos(a), -np.sin(a), np.sin(a)], axis=1)

    kk1 = np.arange(128)[:, None]
    nb = np.arange(128)[None, :]
    a = 2 * np.pi * (kk1 * nb) / 128.0
    fi = np.concatenate([np.cos(a), np.sin(a), -np.sin(a)], axis=1)

    kk2 = np.arange(128)[:, None]
    na = np.arange(64)[None, :]
    g = np.zeros((128, 128, 128), dtype=np.float64)
    for inb in range(128):
        ang = 2 * np.pi * ((inb + 128 * na) * kk2) / N
        g[inb, :, 0:64] = np.cos(ang) / N
        g[inb, :, 64:128] = -np.sin(ang) / N

    f = lambda x: np.ascontiguousarray(x.astype(np.float32))
    g_k2_major = np.transpose(g, (1, 0, 2)).reshape(128, 128 * 128)
    return f(f1), f(tw), f(f2), f(fi), f(g_k2_major)


F1_C, TW_C, F2_C, FI_C, G_C = _consts()


def build_program(debug_taps=(), detect_races=False):
    nc = bass.Bass("TRN2", target_bir_lowering=False,
                   detect_race_conditions=detect_races)
    want_y = "y" in debug_taps

    u_in = nc.dram_tensor("u", [L, DH], F32, kind="ExternalInput").ap()
    k_in = nc.dram_tensor("kk", [DH, L], F32, kind="ExternalInput").ap()
    skip_in = nc.dram_tensor("skip", [1, DH], F32, kind="ExternalInput").ap()
    f1_in = nc.dram_tensor("f1", [128, 256], F32, kind="ExternalInput").ap()
    tw_in = nc.dram_tensor("tw", [128, 384], F32, kind="ExternalInput").ap()
    f2_in = nc.dram_tensor("f2", [128, 384], F32, kind="ExternalInput").ap()
    fi_in = nc.dram_tensor("fi", [128, 384], F32, kind="ExternalInput").ap()
    g_in = nc.dram_tensor("g", [128, 128 * 128], F32, kind="ExternalInput").ap()
    mg_in = nc.dram_tensor("mg", [1, 32], I32, kind="ExternalInput").ap()
    wq_in = nc.dram_tensor("wq", [D, 512], F32, kind="ExternalInput").ap()
    wk_in = nc.dram_tensor("wk", [D, 512], F32, kind="ExternalInput").ap()
    wv_in = nc.dram_tensor("wv", [D, 320], F32, kind="ExternalInput").ap()
    wsd_in = nc.dram_tensor("wsd", [D, 80], F32, kind="ExternalInput").ap()
    wo_in = nc.dram_tensor("wo", [320, D], F32, kind="ExternalInput").ap()
    id_in = nc.dram_tensor("iden", [128, 128], F32, kind="ExternalInput").ap()
    ones_in = nc.dram_tensor("onesd", [128, 1], F32, kind="ExternalInput").ap()
    out = nc.dram_tensor("out", [T, D], F32, kind="ExternalOutput").ap()
    if want_y:
        ydbg = nc.dram_tensor("ydbg", [DH, 8192], F32, kind="ExternalOutput").ap()
    want_c = "cks" in debug_taps
    if want_c:
        cks_d = nc.dram_tensor("cks", [128, 96], F32, kind="ExternalOutput").ap()
    want_i = "idbg" in debug_taps
    if want_i:
        vdbg_r = nc.dram_tensor("vdbg_r", [128, DH * 128], F32, kind="ExternalOutput").ap()
        vdbg_i = nc.dram_tensor("vdbg_i", [128, DH * 128], F32, kind="ExternalOutput").ap()
        zdbg_r = nc.dram_tensor("zdbg_r", [128, 4096], F32, kind="ExternalOutput").ap()
        zdbg_i = nc.dram_tensor("zdbg_i", [128, 4096], F32, kind="ExternalOutput").ap()
        jdbg = nc.dram_tensor("jdbg", [128, 8192], F32, kind="ExternalOutput").ap()
        vdbg_n = nc.dram_tensor("vdbg_n", [128, DH * 128], F32, kind="ExternalOutput").ap()

    vr_d = nc.dram_tensor("vr_d", [128, DH * 128], F32).ap()
    vi_d = nc.dram_tensor("vi_d", [128, DH * 128], F32).ap()
    vn_d = nc.dram_tensor("vn_d", [128, DH * 128], F32).ap()
    xsh = nc.dram_tensor("xsh", [D, 8192], F32, addr_space="Shared").ap()
    flg = nc.dram_tensor("flg", [2, 32], I32, addr_space="Shared").ap()
    ksh = nc.dram_tensor("ksh", [512, 8192], BF16, addr_space="Shared").ap()
    vsh = nc.dram_tensor("vsh", [128, 64 * 528], BF16, addr_space="Shared").ap()

    from contextlib import ExitStack
    with ExitStack() as _st:
        ps = lambda *a: _st.enter_context(nc.psum_tensor(*a))
        sem = lambda n: _st.enter_context(nc.semaphore(n))
        osb = lambda *a, **k: _st.enter_context(nc.sbuf_tensor(*a, **k))
        ones1 = osb("ones1", [1, 512], F32)
        mgs = osb("mgs", [1, 32], I32)
        fls = osb("fls", [1, 32], I32)
        negone = osb("negone", [128, 1], F32)

        _cv = ExitStack()
        sb = lambda *a: _cv.enter_context(nc.sbuf_tensor(*a))
        f1s = sb("f1s", [128, 256], F32R)
        tws = sb("tws", [128, 384], F32)
        f2s = sb("f2s", [128, 384], F32R)
        fis = sb("fis", [128, 384], F32R)
        skrep = sb("skrep", [128, 512], F32)
        skrow = sb("skrow", [1, 512], F32)
        ut0 = sb("ut0", [128, 4096], F32R)
        ut1 = sb("ut1", [128, 4096], F32R)
        jt = sb("jt", [128, 8192], F32R)
        z1r = sb("z1r", [128, 4096], F32R)
        z1i = sb("z1i", [128, 4096], F32R)
        vsl = [[sb(f"v{pl}{bf}", [128, 512], F32) for pl in range(3)]
               for bf in range(4)]
        gsl = [sb(f"gs{i}", [128, 1024], F32R) for i in range(4)]
        ysb = sb("ysb", [64, 8192], F32)
        vjunk = sb("vjunk", [128, 512], F32)
        ysb2 = sb("ysb2", [64, 8192], F32) if "y2" in debug_taps else None
        tmp = sb("tmp", [128, 512], F32)
        cks_sb = sb("cks_sb", [128, 96], F32) if want_c else None

        psum = [ps(f"p{i}", [128, 512], F32) for i in range(8)]

        s_pre = sem("s_pre")
        s_preg = sem("s_preg")
        s_ul = sem("s_ul")
        s_s1 = sem("s_s1")
        s_z1 = sem("s_z1")
        s_x = sem("s_x")
        s_z = sem("s_z")
        s_j = sem("s_j")
        s_je = sem("s_je")
        s_y = sem("s_y")
        s_ye = sem("s_ye")
        s_yd = sem("s_yd")
        s_vd = sem("s_vd")
        s_gd = sem("s_gd")
        s_kv = sem("s_kv")
        s_ka = sem("s_ka")
        s_poll = sem("s_poll")
        s_v = sem("s_v")
        s_bar = sem("s_bar")
        s_fin = sem("s_fin")
        s_y2 = sem("s_y2")
        s_ph = sem("s_ph")
        s_yt = sem("s_yt")
        s_x1 = sem("s_x1")
        s_qk = sem("s_qk")
        s_ev = sem("s_ev")
        s_gt = sem("s_gt")
        s_kw = sem("s_kw")
        s_wl = sem("s_wl")
        s_kt = sem("s_kt")
        s_sc = sem("s_sc")
        s_ex = sem("s_ex")
        s_pv = sem("s_pv")
        s_at = sem("s_at")
        s_ge = sem("s_ge")
        s_op = sem("s_op")
        s_o2 = sem("s_o2")
        s_od = sem("s_od")
        s_mr = sem("s_mr")
        s_st = sem("s_st")
        s_sq = sem("s_sq")
        s_ssm = sem("s_ssm")
        s_rw1 = sem("s_rw1")
        s_rw2 = sem("s_rw2")
        s_g2 = sem("s_g2")
        s_xt = sem("s_xt")
        s_v2 = sem("s_v2")
        block = _st.enter_context(nc.Block())

        jrA = jt.ap()[:, 0:4096]
        jiA = jt.ap()[:, 4096:8192]

        def G1(sub, g):
            return sub * 8 + g + 1

        def GX(sub, sl):
            return sub * 8 + sl + 1

        def GJ(usub, g):
            return usub * 8 + g + 1

        def GY(usub, bk):
            return usub * 16 + bk + 1

        def UL(pair):
            kd = min(pair + 1, 8) * 2
            ud = max(pair + 1 - 8, 0) * 8
            return (kd + ud) * 16

        NPRE_G = 4

        @block.sync
        def _(sync):
            sync.dma_start(out=tws[:], in_=tw_in[:]).then_inc(s_pre, 16)
            sync.dma_start(out=skrow[:], in_=skip_in[:]).then_inc(s_pre, 16)
            sync.dma_start(out=mgs[:], in_=mg_in[:]).then_inc(s_pre, 16)

            if want_y and "y2" in debug_taps:
                pass
            for n in range(128):
                usub, sl = n // 8, n % 8
                co = usub * 32 + sl * 4
                if n >= 4:
                    sync.wait_ge(s_z, 128 + n - 3)
                else:
                    sync.wait_ge(s_kv, NKSUB * 8 * 3 * 16)
                for pl, src in enumerate((vr_d, vi_d, vn_d)):
                    sync.dma_start(
                        out=vsl[n % 4][pl][:],
                        in_=src[:, co * 128:(co + 4) * 128],
                    ).then_inc(s_vd, 16)
            for pl in range(3):
                sync.dma_start(out=vjunk[:],
                               in_=vr_d[:, 0:512]).then_inc(s_vd, 16)


        @block.gpsimd
        def _(gpsimd):
            gpsimd.dma_start(out=f1s[:], in_=f1_in[:]).then_inc(s_preg, 16)
            gpsimd.dma_start(out=f2s[:], in_=f2_in[:]).then_inc(s_preg, 16)
            gpsimd.dma_start(out=fis[:], in_=fi_in[:]).then_inc(s_preg, 16)

            ksrc = k_in.rearrange("c (n2 n1) -> n2 c n1", n2=64)
            usrc = u_in.rearrange("(n2 n1) d -> n2 n1 d", n2=64)

            def load_pair(pair):
                if pair >= 2:
                    lsub = 2 * (pair - 2) + 1
                    gpsimd.wait_ge(s_s1, G1(lsub, 7))
                dst = ut0 if pair % 2 == 0 else ut1
                if pair < 8:
                    for h in range(2):
                        gpsimd.dma_start(
                            out=dst[h * 64:(h + 1) * 64, :],
                            in_=ksrc[:, pair * 64 + h * 32:pair * 64 + h * 32 + 32, :],
                        ).then_inc(s_ul, 16)
                else:
                    p = pair - 8
                    for h in range(2):
                        for qq in range(4):
                            gpsimd.dma_start(
                                out=dst[h * 64:(h + 1) * 64,
                                        qq * 1024:(qq + 1) * 1024],
                                in_=usrc[:, qq * 32:(qq + 1) * 32,
                                         p * 64 + h * 32:p * 64 + h * 32 + 32],
                            ).then_inc(s_ul, 16)

            for pair in range(10):
                load_pair(pair)
            for n in range(NUSUB * 16):
                usub, bk = n // 16, n % 16
                if n >= 4:
                    gpsimd.wait_ge(s_y, n - 3)
                gpsimd.dma_start(
                    out=gsl[n % 4][:],
                    in_=g_in.rearrange("a (nb c) -> a nb c", nb=128)
                        [:, bk * 8:(bk + 1) * 8, :],
                ).then_inc(s_gd, 16)
                if bk == 0 and usub % 2 == 1:
                    pair = 10 + usub // 2
                    if pair <= 15:
                        load_pair(pair)
                        if pair == 15:
                            gpsimd.dma_start(out=vjunk[0:64, 0:512],
                                             in_=usrc[:, 0:32, 0:16],
                                             ).then_inc(s_ul, 16)
            gpsimd.dma_start(out=vjunk[:],
                             in_=g_in[:, 0:512]).then_inc(s_gd, 16)

            if "noxsh" not in debug_taps:
                gpsimd.wait_ge(s_yd, NUSUB // 2 * 16)
                if "bar1poll" in debug_taps:
                    _barrier(nc, gpsimd, flg, mgs, fls, s_poll, 0, nopoll=True)
                elif "nobar" not in debug_taps:
                    _barrier(nc, gpsimd, flg, mgs, fls, s_poll, 0)
            gpsimd.wait_ge(s_gd, (NUSUB * 16 + 1) * 16)
            gpsimd.wait_ge(s_ul, UL(15) + 16)
            gpsimd.wait_ge(s_preg, 3 * 16)
            gpsimd.wait_ge(s_vd, (128 + 1) * 3 * 16)
            gpsimd.wait_ge(s_pre, 3 * 16)
            gpsimd.nop().then_inc(s_bar, 1)
            gpsimd.nop().then_inc(s_ph, 1)

        @block.tensor
        def _(tensor):
            tensor.wait_ge(s_pre, 3 * 16)
            tensor.wait_ge(s_preg, 3 * 16)
            tensor.wait_ge(s_fin, 1)
            for qq in range(4):
                mm = tensor.matmul(psum[7][:, qq * 128:(qq + 1) * 128],
                                   ones1[0:1, 0:128],
                                   skrow[:, qq * 128:(qq + 1) * 128], start=True,
                                   stop=True, skip_group_check=True)
            mm.then_inc(s_fin, 1)

            def stage1(sub, tile, pbase, kind):
                for g in range(8):
                    bs = (g % 2) * 2
                    sR, sI = psum[bs], psum[bs + 1]
                    gid = G1(sub, g)
                    if gid - 2 > 0:
                        tensor.wait_ge(s_z1, gid - 2)
                    pb = slice(pbase, pbase + 64)
                    for q in range(4):
                        d = g * 4 + q
                        if kind == "k":
                            lhsT = tile[:, :].rearrange(
                                "p (c n1) -> p c n1", c=32)[pb, d, :]
                        else:
                            lhsT = tile[:, :].rearrange(
                                "p (n1 d) -> p n1 d", d=32)[pb, :, d]
                        tensor.matmul(sR[:, q * 128:(q + 1) * 128], lhsT,
                                      f1s[pb, 0:128], start=True, stop=True,
                                      skip_group_check=True)
                        tensor.matmul(sI[:, q * 128:(q + 1) * 128], lhsT,
                                      f1s[pb, 128:256], start=True, stop=True,
                                      skip_group_check=True)
                    tensor.drain().then_inc(s_s1, 1)

            def stage2(sub):
                for sl in range(8):
                    bx = 4 + (sl % 2) * 2
                    xR, xI = psum[bx], psum[bx + 1]
                    gid = GX(sub, sl)
                    if gid - 2 > 0:
                        tensor.wait_ge(s_z, gid - 2)
                        if gid - 2 <= NKSUB * 8:
                            tensor.wait_ge(s_ka, gid - 2)
                    tensor.wait_ge(s_z1, G1(sub, sl))
                    cs = slice(sl * 512, (sl + 1) * 512)
                    tensor.matmul(xR[:], f2s[:, 0:128], z1r[:, cs], start=True,
                                  stop=False, skip_group_check=True)
                    tensor.matmul(xR[:], f2s[:, 256:384], z1i[:, cs], start=False,
                                  stop=True, skip_group_check=True)
                    tensor.matmul(xI[:], f2s[:, 128:256], z1r[:, cs], start=True,
                                  stop=False, skip_group_check=True)
                    tensor.matmul(xI[:], f2s[:, 0:128], z1i[:, cs],
                                  start=False, stop=True, skip_group_check=True)
                    tensor.drain().then_inc(s_x, 1)

            def istage1(usub):
                for g in range(8):
                    bs = 4 + (g % 2) * 2
                    bR, bI = psum[bs], psum[bs + 1]
                    gid = GJ(usub, g)
                    if gid - 2 > 0:
                        tensor.wait_ge(s_je, gid - 2)
                    for q in range(4):
                        d = g * 4 + q
                        zR = z1r[:, d * 128:(d + 1) * 128]
                        zI = z1i[:, d * 128:(d + 1) * 128]
                        c = slice(q * 128, (q + 1) * 128)
                        tensor.matmul(bR[:, c], zR, fis[:, 0:128], start=True,
                                      stop=False, skip_group_check=True)
                        tensor.matmul(bR[:, c], zI, fis[:, 256:384], start=False,
                                      stop=True, skip_group_check=True)
                        tensor.matmul(bI[:, c], zR, fis[:, 128:256], start=True,
                                      stop=False, skip_group_check=True)
                        tensor.matmul(bI[:, c], zI, fis[:, 0:128],
                                      start=False, stop=True,
                                      skip_group_check=True)
                    tensor.drain().then_inc(s_j, 1)

            def istage2(usub):
                jrap = jrA.rearrange("a (s nb) -> a nb s", nb=128)
                jiap = jiA.rearrange("a (s nb) -> a nb s", nb=128)
                tensor.wait_ge(s_je, (usub + 1) * 8)
                for bk in range(16):
                    yb = psum[bk % 2]
                    gs = gsl[(usub * 16 + bk) % 4][:, :].rearrange(
                        "a (nb c) -> a nb c", nb=8)
                    tensor.wait_ge(s_gd, (GY(usub, bk) + 1) * 16)
                    gid = GY(usub, bk)
                    if gid - 2 > 0:
                        tensor.wait_ge(s_ye, gid - 2)
                    for q in range(8):
                        inb = bk * 8 + q
                        c = slice(q * 64, (q + 1) * 64)
                        tensor.matmul(yb[0:32, c], jrap[:, inb, :],
                                      gs[:, q, 0:64], start=True,
                                      stop=False, skip_group_check=True)
                        tensor.matmul(yb[0:32, c], jiap[:, inb, :],
                                       gs[:, q, 64:128], start=False,
                                       stop=True, skip_group_check=True)
                    tensor.drain().then_inc(s_y, 1)

            tensor.wait_ge(s_fin, 3)
            for sub in range(NSUB):
                pair = sub // 2
                half = sub % 2
                tensor.wait_ge(s_ul, UL(pair))
                tile = (ut0 if pair % 2 == 0 else ut1).ap()
                if sub < NKSUB:
                    stage1(sub, tile, half * 64, "k")
                    stage2(sub)
                else:
                    usub = sub - NKSUB
                    if usub >= 1:
                        tensor.wait_ge(s_ye, usub * 16)
                    stage1(sub, tile, half * 64, "u")
                    if usub >= 1:
                        tensor.wait_ge(s_je, usub * 8)
                    stage2(sub)
                    tensor.wait_ge(s_z, GX(sub, 7))
                    istage1(usub)
                    istage2(usub)
            tensor.drain().then_inc(s_ph, 1)

        @block.vector
        def _(vector):
            vector.memset(ones1[:], 1.0)
            vector.memset(negone[:], -1.0)
            vector.wait_ge(s_pre, 3 * 16)
            vector.tensor_scalar(skrow[:], skrow[:], 1.0, None,
                                 ALU.add).then_inc(s_fin, 1)
            vector.wait_ge(s_fin, 2)
            vector.tensor_copy(skrep[:], psum[7][:, 0:512]).then_inc(s_fin, 1)

            twb = tws.ap().rearrange("a (pl c) -> a pl c", pl=3)
            vcnt = [0]

            def vsync(inst):
                vcnt[0] += 1
                inst.then_inc(s_v, 1)
                vector.wait_ge(s_v, vcnt[0])

            def bct(pl):
                return twb[:, pl, :].unsqueeze(1).to_broadcast((128, 4, 128))

            def tw1(sub):
                for g in range(8):
                    bs = (g % 2) * 2
                    sR = psum[bs][:].rearrange("a (s c) -> a s c", s=4)
                    sI = psum[bs + 1][:].rearrange("a (s c) -> a s c", s=4)
                    vector.wait_ge(s_s1, G1(sub, g))
                    if 1 <= sub <= NKSUB:
                        vector.wait_ge(s_x, GX(sub - 1, g))
                    elif sub > NKSUB:
                        vector.wait_ge(s_j, GJ(sub - 1 - NKSUB, g))
                    cs = slice(g * 512, (g + 1) * 512)
                    zrv = z1r[:, cs].rearrange("a (s c) -> a s c", s=4)
                    ziv = z1i[:, cs].rearrange("a (s c) -> a s c", s=4)
                    tm = tmp[:].rearrange("a (s c) -> a s c", s=4)
                    vector.tensor_tensor(zrv, sR, bct(0), ALU.mult)
                    vsync(vector.tensor_tensor(tm, sI, bct(2), ALU.mult))
                    vsync(vector.tensor_tensor(zrv, zrv, tm, ALU.add))
                    vector.tensor_tensor(ziv, sR, bct(1), ALU.mult)
                    vsync(vector.tensor_tensor(tm, sI, bct(0), ALU.mult))
                    vector.tensor_tensor(ziv, ziv, tm, ALU.add).then_inc(s_z1, 1)

            def pw_k(sub):
                for sl in range(8):
                    bx = 4 + (sl % 2) * 2
                    xR, xI = psum[bx], psum[bx + 1]
                    vector.wait_ge(s_x, GX(sub, sl))
                    n = sub * 8 + sl
                    if n >= 2:
                        vector.wait_ge(s_kv, (n - 1) * 3 * 16)
                    ch0 = sub * 32 + sl * 4
                    rep = skrep[:, ch0:ch0 + 4]
                    repb = rep.unsqueeze(2).to_broadcast((128, 4, 128))
                    xRv = xR[:].rearrange("a (s c) -> a s c", s=4)
                    vb = vsl[sl % 2]
                    vector.tensor_tensor(
                        vb[0][:].rearrange("a (s c) -> a s c", s=4), xRv, repb,
                        ALU.add)
                    vector.tensor_copy(vb[1][:], xI[:]).then_inc(s_z, 1)

            def pw_u(sub, usub):
                for sl in range(8):
                    bx = 4 + (sl % 2) * 2
                    xR, xI = psum[bx], psum[bx + 1]
                    n = usub * 8 + sl
                    vector.wait_ge(s_x, GX(sub, sl))
                    vector.wait_ge(s_vd, (n + 2) * 3 * 16)
                    cs = slice(sl * 512, (sl + 1) * 512)
                    vb = vsl[n % 4]
                    vector.tensor_tensor(z1r[:, cs], xR[:], vb[0][:], ALU.mult)
                    vsync(vector.tensor_tensor(tmp[:], xI[:], vb[2][:], ALU.mult))
                    vsync(vector.tensor_tensor(z1r[:, cs], z1r[:, cs], tmp[:],
                                               ALU.add))
                    vector.tensor_tensor(z1i[:, cs], xR[:], vb[1][:], ALU.mult)
                    vsync(vector.tensor_tensor(tmp[:], xI[:], vb[0][:], ALU.mult))
                    vector.tensor_tensor(z1i[:, cs], z1i[:, cs], tmp[:],
                                         ALU.add).then_inc(s_z, 1)

            for sub in range(NSUB):
                tw1(sub)
                if sub < NKSUB:
                    pw_k(sub)
                else:
                    usub = sub - NKSUB
                    if want_c:
                        vector.reduce_sum(cks_sb[:, 3 * usub:3 * usub + 1],
                                          z1r[:].bitcast(F32),
                                          axis=mybir.AxisListType.X)
                    pw_u(sub, usub)
                    if "jedve" in debug_taps:
                        for g in range(8):
                            bs = 4 + (g % 2) * 2
                            vector.wait_ge(s_j, GJ(usub, g))
                            c0 = g * 512
                            vector.tensor_copy(jrA[:, c0:c0 + 512], psum[bs][:])
                            vector.tensor_copy(jiA[:, c0:c0 + 512],
                                               psum[bs + 1][:]).then_inc(s_je, 1)
                    if "y2" in debug_taps and usub % 2 == 1:
                        pr = usub // 2
                        vector.wait_ge(s_ye, (usub + 1) * 16)
                        if pr >= 1:
                            vector.wait_ge(s_fin, 3 + pr * 16)
                        vector.tensor_copy(ysb2[:, :], ysb[:, :]).then_inc(s_y2, 1)
                    if want_c:
                        vector.reduce_sum(cks_sb[:, 3 * usub + 1:3 * usub + 2],
                                          z1r[:].bitcast(F32),
                                          axis=mybir.AxisListType.X)
                        vector.wait_ge(s_je, (usub + 1) * 8)
                        vector.reduce_sum(cks_sb[:, 3 * usub + 2:3 * usub + 3],
                                          jt[:].bitcast(F32),
                                          axis=mybir.AxisListType.X)
            vector.nop().then_inc(s_ph, 1)

        @block.scalar
        def _(scalar):
            for sub in range(NKSUB):
                for sl in range(8):
                    bx = 4 + (sl % 2) * 2
                    xI = psum[bx + 1]
                    n = sub * 8 + sl
                    scalar.wait_ge(s_z, GX(sub, sl))
                    vb = vsl[sl % 2]
                    scalar.activation(vb[2][:], xI[:], AF.Copy,
                                      scale=negone[:, 0:1]).then_inc(s_ka, 1)
                    ch0 = sub * 32 + sl * 4
                    for pl, dst in enumerate((vr_d, vi_d, vn_d)):
                        scalar.dma_start(
                            out=dst[:, ch0 * 128:(ch0 + 4) * 128],
                            in_=vb[pl][:]).then_inc(s_kv, 16)

            for usub in range(NUSUB):
                p, half = usub // 2, usub % 2

                def ship_pair(pp):
                    scalar.wait_ge(s_y, NUSUB * 16 if pp >= NUSUB // 2 - 1
                                   else GY(2 * pp + 2, 1))
                    if "noxsh" not in debug_taps:
                        for arm in nc.switch_core_id(engines=[scalar], n=128):
                            q = arm.logical & 1
                            scalar.dma_start(
                                out=xsh[q * 512 + pp * 64:
                                        q * 512 + (pp + 1) * 64, :],
                                in_=ysb[:, :]).then_inc(s_yd, 16)
                    if want_y:
                        scalar.dma_start(out=ydbg[pp * 64:(pp + 1) * 64, :],
                                         in_=ysb[:, :]).then_inc(s_fin, 16)

                if "jedve" not in debug_taps:
                    for g in range(8):
                        bs = 4 + (g % 2) * 2
                        scalar.wait_ge(s_j, GJ(usub, g))
                        c0 = g * 512
                        scalar.copy(jrA[:, c0:c0 + 512], psum[bs][:])
                        scalar.copy(jiA[:, c0:c0 + 512],
                                    psum[bs + 1][:]).then_inc(s_je, 1)
                if half == 0 and p >= 1:
                    ship_pair(p - 1)
                    if "noxsh" not in debug_taps:
                        scalar.wait_ge(s_yd, p * 16)
                    if want_y:
                        scalar.wait_ge(s_fin, 3 + p * 16)
                for bk in range(16):
                    scalar.wait_ge(s_y, GY(usub, bk))
                    scalar.copy(
                        ysb[(usub % 2) * 32:(usub % 2) * 32 + 32,
                            bk * 512:(bk + 1) * 512],
                        psum[bk % 2][0:32, :]).then_inc(s_ye, 1)
            ship_pair(NUSUB // 2 - 1)
            if want_i:
                scalar.wait_ge(s_y, NUSUB * 16)
                scalar.dma_start(out=vdbg_r[:], in_=vr_d[:]).then_inc(s_kv, 16)
                scalar.dma_start(out=vdbg_i[:], in_=vi_d[:]).then_inc(s_kv, 16)
                scalar.dma_start(out=vdbg_n[:], in_=vn_d[:]).then_inc(s_kv, 16)
                scalar.dma_start(out=zdbg_r[:], in_=z1r[:].bitcast(F32)).then_inc(s_kv, 16)
                scalar.dma_start(out=zdbg_i[:], in_=z1i[:].bitcast(F32)).then_inc(s_kv, 16)
                scalar.dma_start(out=jdbg[:], in_=jt[:].bitcast(F32)).then_inc(s_kv, 16)
                scalar.wait_ge(s_kv, (NKSUB * 8 * 3 + 6) * 16)
            if want_y and "y2" in debug_taps:
                pass
            if want_c:
                scalar.wait_ge(s_z, NSUB * 8)
                scalar.dma_start(out=cks_d[:], in_=cks_sb[:]).then_inc(s_kv, 16)
            scalar.wait_ge(s_bar, 1)
            scalar.wait_ge(s_yd, (NUSUB // 2) * 16)
            if want_y:
                scalar.wait_ge(s_fin, 3 + 8 * 16)
            scalar.nop().then_inc(s_ph, 1)


        if "noattn" in debug_taps:
            _cv.close()
            return nc
        _cv.close()
        at = ExitStack()
        ab = lambda *a, **k: at.enter_context(
            nc.sbuf_tensor(*a, unsafe_allow_realloc=True, **k))

        ytb = [ab(f"ytb{i}", [128, 512], F32R) for i in range(8)]
        x1tb = [[ab(f"x1_{b}_{d}", [128, 512], BF16) for d in range(8)]
                for b in range(2)]
        Qt = [ab(f"qt{i}", [128, 4096], BF16) for i in range(4)]
        wqk_sb = ab("wqk", [128, 8192], BF16)
        wvsd_sb = ab("wvsd", [128, 3200], BF16)
        wo_sb = ab("wo_sb", [128, 3072], F32R)
        kstage = [ab(f"kst{i}", [128, 512], BF16) for i in range(2)]
        v_tm = [ab(f"vtm{i}", [128, 528], BF16) for i in range(2)]
        kgb = [ab(f"kgb{i}", [128, 1024], BF16) for i in range(4)]
        vgb = [ab(f"vgb{i}", [128, 528], BF16) for i in range(4)]
        exp_sb = [ab(f"exps{i}", [128, 512], BF16) for i in range(2)]
        att_sb = [ab(f"atts{i}", [128, 2112], F32) for i in range(2)]
        ge_sb = [ab(f"ges{i}", [128, 1280], F32) for i in range(2)]
        geT_sb = ab("geT", [128, 384], F32R)
        x1n = ab("x1n", [128, 1024], F32)
        osb = ab("osb", [128, 1024], F32)
        out_sb = [ab(f"outs{i}", [128, 1024], F32) for i in range(4)]
        ytb2 = [ab(f"ytb2_{i}", [128, 128], F32R) for i in range(4)]
        ajunk = ab("ajunk", [128, 512], F32R)
        aj2 = ab("aj2", [128, 1024], BF16)
        murow = ab("murow", [1, 512], F32)
        m2row = ab("m2row", [1, 512], F32)
        varow = ab("varow", [1, 512], F32)
        rorow = ab("rorow", [1, 512], F32)
        mrcol_sb = ab("mrcol", [128, 64], F32)
        gl_sb = ab("gl_sb", [128, 80], F32)

        vt40 = ab("vt40", [128, 2560], F32)
        vt80 = ab("vt80", [128, 1280], F32)

        rk_sb = ab("rk_sb", [128, 80], F32)

        sig_t = ab("sig_t", [128, 40], F32)
        wv_sb = ab("wv_sb", [128, 40], F32)
        msk_sb = ab("msk_sb", [128, 1280], F32)
        zr8 = ab("zr8", [128, 8], F32)
        mt_sb = ab("mt_sb", [128, 40], F32)

        onescol = ab("onescol", [128, 1], F32R)
        onescf = ab("onescf", [128, 1], F32)
        zcol = ab("zcol", [128, 1], F32)
        idn = ab("idn", [128, 128], F32)
        idnr = ab("idnr", [128, 128], F32R)
        sc8 = ab("sc8", [128, 1], F32)
        neghalf = ab("neghalf", [128, 1], F32)
        epsc = ab("epsc", [128, 1], F32)
        ln2row = ab("ln2row", [128, 8], F32)
        sqt = [ge_sb[0].ap()[:, 0:512], ge_sb[0].ap()[:, 512:1024]]
        xv_sb = ge_sb[1].ap()[:, 0:320]
        rkw = ge_sb[1].ap()[:, 320:640]
        sig2 = ge_sb[1].ap()[:, 640:680]
        sig3 = ge_sb[1].ap()[:, 680:720]

        LVL = 99
        for _t in debug_taps:
            if _t.startswith("lvl") and _t[3:].isdigit():
                LVL = int(_t[3:])
        if "phb" in debug_taps:
            LVL = min(LVL, 8)
        NTCE = 8 if LVL >= 6 else 1
        E_W = LVL >= 1
        E_A = LVL >= 2
        E_B = LVL >= 3
        E_C = LVL >= 9
        want_q = "qkv" in debug_taps
        if want_q:
            kdbg = nc.dram_tensor("kdbg", [512, 8192], BF16, kind="ExternalOutput").ap()
            vdbg2 = nc.dram_tensor("vdbg2", [128, 64 * 528], BF16, kind="ExternalOutput").ap()
            qdbg = nc.dram_tensor("qdbg", [128, 16384], BF16, kind="ExternalOutput").ap()
            mdbg = nc.dram_tensor("mdbg", [128, 1280], F32, kind="ExternalOutput").ap()
            mrdbg = nc.dram_tensor("mrdbg", [128, 64], F32, kind="ExternalOutput").ap()

        NTC = 8
        def YTN(tc, dc):
            return tc * 9 + dc

        def EVB(tc):
            return tc * 17

        def QKC(tc, m):
            return tc * 12 + m + 1

        def GT(tc, it):
            return tc * 4 + it + 1

        def KWB(tc):
            return tc * 128

        def JBC(ib, h, jb):
            return (ib * 8 + h) * 64 + jb + 1

        def KVN(ib, h, g):
            return (ib * 8 + h) * 8 + g

        def OPN(gt_idx, k):
            return gt_idx * 3 + k

        def Y2N(gt_idx, dc):
            return 72 + gt_idx * 9 + dc

        if "lvlA" in debug_taps:
            at.close()
            return nc
        @block.sync
        def _(sync):
            sync.wait_ge(s_ph, 4)
            if not E_C:
                return
            sync.wait_ge(s_bar, 2)
            for ib in range(8):
                for h in range(8):
                    for g in range(8):
                        n = KVN(ib, h, g)
                        if n >= 4:
                            pn = n - 4
                            ibh2, g2 = divmod(pn, 8)
                            sync.wait_ge(s_pv, ibh2 * 64 + g2 * 8 + 8)
                        for hf in range(2):
                            sync.dma_start(out=kgb[n % 4][hf * 64:hf * 64 + 64, :],
                                           in_=ksh[h * 64:(h + 1) * 64,
                                                   g * 1024:(g + 1) * 1024]
                                           ).then_inc(s_kt, 16)
                        sync.dma_start(
                            out=vgb[n % 4][:].rearrange(
                                "p (t x) -> p t x", t=8),
                            in_=vsh.rearrange("p (t x) -> p t x", t=64)
                                [:, g * 8:(g + 1) * 8, h * 66:h * 66 + 66],
                        ).then_inc(s_kt, 16)
            for _d in range(3):
                sync.dma_start(out=aj2[0:64, :],
                               in_=ksh[0:64, 0:1024]).then_inc(s_kt, 16)

        @block.gpsimd
        def _(gpsimd):
            gpsimd.wait_ge(s_ph, 4)
            if not E_W:
                return
            gpsimd.dma_start(out=idn[:], in_=id_in[:]).then_inc(s_wl, 16)
            gpsimd.dma_start(out=idnr[:], in_=id_in[:]).then_inc(s_wl, 16)
            gpsimd.dma_start(out=onescol[:], in_=ones_in[:]).then_inc(s_preg, 16)
            for dc in range(8):
                gpsimd.dma_start(out=wqk_sb[:, dc * 1024:dc * 1024 + 512],
                                 in_=wq_in[dc * 128:(dc + 1) * 128, :]
                                 ).then_inc(s_wl, 16)
                gpsimd.dma_start(out=wqk_sb[:, dc * 1024 + 512:(dc + 1) * 1024],
                                 in_=wk_in[dc * 128:(dc + 1) * 128, :]
                                 ).then_inc(s_wl, 16)
            for dc in range(8):
                gpsimd.dma_start(out=wvsd_sb[:, dc * 400:dc * 400 + 320],
                                 in_=wv_in[dc * 128:(dc + 1) * 128, :]
                                 ).then_inc(s_wl, 16)
                gpsimd.dma_start(out=wvsd_sb[:, dc * 400 + 320:(dc + 1) * 400],
                                 in_=wsd_in[dc * 128:(dc + 1) * 128, :]
                                 ).then_inc(s_wl, 16)
            for ch in range(3):
                rows = 128 if ch < 2 else 64
                gpsimd.dma_start(out=wo_sb[0:rows, ch * 1024:(ch + 1) * 1024],
                                 in_=wo_in[ch * 128:ch * 128 + rows, :]
                                 ).then_inc(s_wl, 16)

            for tc in range((NTCE + 1) if E_A else 0):
                if tc < NTCE:
                    if tc >= 1:
                        gpsimd.wait_ge(s_x1, tc * 8)
                    for arm in nc.switch_core_id(engines=[gpsimd], n=128):
                        lam = arm.logical & 1
                        for dc in range(8):
                            gpsimd.dma_start(
                                out=ytb[dc][:],
                                in_=xsh[dc * 128:(dc + 1) * 128,
                                        lam * 4096 + tc * 512:
                                        lam * 4096 + (tc + 1) * 512],
                            ).then_inc(s_yt, 16)
                    gpsimd.dma_start(out=ajunk[:],
                                     in_=xsh[0:128, 0:512]).then_inc(s_yt, 16)
                if tc >= 1 and E_B:
                    ptc = tc - 1
                    for arm in nc.switch_core_id(engines=[gpsimd], n=128):
                        lam = arm.logical & 1
                        for cc in range(4):
                            gpsimd.wait_ge(s_ev, EVB(ptc) + cc + 7)
                            gpsimd.dma_start(
                                out=ksh[cc * 128:(cc + 1) * 128,
                                        lam * 4096 + ptc * 512:
                                        lam * 4096 + (ptc + 1) * 512],
                                in_=kstage[cc % 2][:]).then_inc(s_kw, 16)
                        for it in range(4):
                            gpsimd.wait_ge(s_gt, GT(ptc, it))
                            col0 = (lam * 32 + ptc * 4 + it) * 528
                            gpsimd.dma_start(
                                out=vsh[:, col0:col0 + 528],
                                in_=v_tm[it % 2][:]).then_inc(s_kw, 16)
            if LVL >= 6 and E_B:
                gpsimd.wait_ge(s_kw, 8 * 128)
                _barrier(nc, gpsimd, flg, mgs, fls, s_poll, 1)
            gpsimd.nop().then_inc(s_bar, 1)

            if want_q:
                gpsimd.dma_start(out=kdbg[:], in_=ksh[:]).then_inc(s_wl, 16)
                gpsimd.dma_start(out=vdbg2[:], in_=vsh[:]).then_inc(s_wl, 16)
                for i in range(4):
                    gpsimd.dma_start(out=qdbg[:, i * 4096:(i + 1) * 4096],
                                     in_=Qt[i][:]).then_inc(s_wl, 16)
                gpsimd.dma_start(out=mdbg[:], in_=msk_sb[:]).then_inc(s_wl, 16)
                gpsimd.dma_start(out=mrdbg[:], in_=mrcol_sb[:]).then_inc(s_wl, 16)
                gpsimd.wait_ge(s_wl, (2 + 35 + 8) * 16)

            if not E_C:
                return
            for ib in range(8):
                if ib >= 1:
                    for it in range(4):
                        pgt = (ib - 1) * 4 + it
                        gpsimd.wait_ge(s_o2, pgt + 1)
                        gpsimd.dma_start(
                            out=out[pgt * 128:(pgt + 1) * 128, :],
                            in_=out_sb[pgt % 4][:]).then_inc(s_od, 16)
                for it in range(4):
                    gt_idx = ib * 4 + it
                    for arm in nc.switch_core_id(engines=[gpsimd], n=128):
                        lam = arm.logical & 1
                        for dc in range(8):
                            n2 = gt_idx * 8 + dc
                            if n2 >= 4:
                                gpsimd.wait_ge(s_xt, n2 - 3)
                            t0 = lam * 4096 + gt_idx * 128
                            gpsimd.dma_start(
                                out=ytb2[n2 % 4][:],
                                in_=xsh[dc * 128:(dc + 1) * 128, t0:t0 + 128],
                            ).then_inc(s_yt, 16)
                    gpsimd.dma_start(out=ajunk[:, 0:128],
                                     in_=xsh[0:128, 0:128]).then_inc(s_yt, 16)
            for it in range(4):
                pgt = 7 * 4 + it
                gpsimd.wait_ge(s_o2, pgt + 1)
                gpsimd.dma_start(out=out[pgt * 128:(pgt + 1) * 128, :],
                                 in_=out_sb[pgt % 4][:]).then_inc(s_od, 16)
            gpsimd.wait_ge(s_od, 32 * 16)

        @block.tensor
        def _(tensor):
            tensor.wait_ge(s_ph, 4)
            if not E_A:
                return
            tensor.wait_ge(s_rw1, 1)
            tensor.wait_ge(s_preg, 4 * 16)
            for tc in range(NTCE):
                if tc >= 1:
                    tensor.wait_ge(s_x1, tc * 8)
                for dc in range(8):
                    tensor.wait_ge(s_yt, (YTN(tc, dc) + 2) * 16)
                    tensor.matmul(psum[5][0:1, :], onescf[:, :],
                                  ytb[dc][:].bitcast(F32),
                                  start=(dc == 0), stop=(dc == 7),
                                  skip_group_check=True)
                tensor.drain().then_inc(s_st, 1)
                if tc >= 1:
                    tensor.wait_ge(s_ev, EVB(tc - 1) + 1)
                for dc in range(8):
                    tensor.wait_ge(s_sq, tc * 8 + dc + 1)
                    mm = tensor.matmul(psum[6][0:1, :], onescf[:, :],
                                       sqt[dc % 2][:], start=(dc == 0),
                                       stop=(dc == 7), skip_group_check=True)
                    mm.then_inc(s_ssm, 1)
                tensor.drain().then_inc(s_st, 1)
                tensor.wait_ge(s_rw1, tc + 2)
                tensor.matmul(psum[7][:, :], ones1[0:1, 0:128], murow[:, :],
                              start=True, stop=True, skip_group_check=True)
                tensor.wait_ge(s_rw2, tc + 1)
                tensor.matmul(psum[5][:, :], ones1[0:1, 0:128], rorow[:, :],
                              start=True, stop=True, skip_group_check=True)
                for it in range(4):
                    tensor.transpose(psum[6][:, it * 2:it * 2 + 1],
                                     murow[0:1, it * 128:(it + 1) * 128],
                                     idn[0:1, 0:1])
                    tensor.transpose(psum[6][:, it * 2 + 1:it * 2 + 2],
                                     rorow[0:1, it * 128:(it + 1) * 128],
                                     idn[0:1, 0:1])
                tensor.drain().then_inc(s_st, 1)

                if not E_B:
                    continue
                tensor.wait_ge(s_x1, (tc + 1) * 8)
                if tc == 0:
                    tensor.wait_ge(s_wl, (2 + 16) * 16)
                for m in range(8):
                    qk, cc = m // 4, m % 4
                    bank = m % 4
                    if m >= 4:
                        tensor.wait_ge(s_ev, EVB(tc) + m - 1)
                    elif tc >= 1:
                        tensor.wait_ge(s_ev, EVB(tc - 1) + m + 7)
                    for dc in range(8):
                        tensor.matmul(
                            psum[bank][:, :],
                            wqk_sb[:, dc * 1024 + qk * 512 + cc * 128:
                                   dc * 1024 + qk * 512 + (cc + 1) * 128],
                            x1tb[tc % 2][dc][:],
                            start=(dc == 0), stop=(dc == 7),
                            skip_group_check=True)
                    tensor.drain().then_inc(s_qk, 1)
                if tc == 0:
                    tensor.wait_ge(s_wl, (2 + 32) * 16)
                for it in range(4):
                    if tc > 0 or it > 0:
                        tensor.wait_ge(s_gt, GT(tc, it) - 1)
                        pe = EVB(tc) + 10 + 2 * (it - 1) if it > 0 else \
                            EVB(tc - 1) + 10 + 2 * 3
                        tensor.wait_ge(s_ev, pe)
                    for dc in range(8):
                        tensor.matmul(
                            psum[4][:, 0:400],
                            x1tb[tc % 2][dc][:, it * 128:(it + 1) * 128],
                            wvsd_sb[:, dc * 400:(dc + 1) * 400],
                            start=(dc == 0), stop=(dc == 7),
                            skip_group_check=True)
                    tensor.drain().then_inc(s_qk, 1)

            if not E_C:
                return
            tensor.wait_ge(s_bar, 2)
            tensor.wait_ge(s_ev, NTC * 17)
            tensor.wait_ge(s_gt, NTC * 4)
            for ib in range(8):
                for h in range(8):
                    ibh = ib * 8 + h
                    tensor.wait_ge(s_at, ibh)
                    for jb in range(64):
                        g = jb // 8
                        n = KVN(ib, h, g)
                        tensor.wait_ge(s_kt, (n + 2) * 48)
                        if JBC(ib, h, jb) - 2 > 0:
                            tensor.wait_ge(s_ex, JBC(ib, h, jb) - 2)
                        hb = (h % 2) * 64
                        tensor.matmul(
                            psum[jb % 2][:, :],
                            kgb[n % 4][hb:hb + 64,
                                       (jb % 8) * 128:(jb % 8 + 1) * 128],
                            Qt[h // 2][hb:hb + 64, ib * 512:(ib + 1) * 512],
                            start=True, stop=True, skip_group_check=True)
                        tensor.drain().then_inc(s_sc, 1)
                        tensor.wait_ge(s_ex, JBC(ib, h, jb))
                        for it in range(4):
                            mm = tensor.matmul(
                                psum[2][:, it * 66:it * 66 + 66],
                                exp_sb[jb % 2][:, it * 128:(it + 1) * 128],
                                vgb[n % 4][:, (jb % 8) * 66:(jb % 8 + 1) * 66],
                                start=(jb == 0), stop=(jb == 63),
                                skip_group_check=True)
                        if jb == 63:
                            tensor.drain().then_inc(s_pv, 1)
                        else:
                            mm.then_inc(s_pv, 1)
                for it in range(4):
                    gt_idx = ib * 4 + it
                    tensor.wait_ge(s_ge, gt_idx + 1)
                    if gt_idx >= 1:
                        tensor.wait_ge(s_o2, gt_idx)
                    for k in range(3):
                        cols = 128 if k < 2 else 64
                        tensor.transpose(
                            psum[3][0:cols, k * 128:k * 128 + 128],
                            ge_sb[ib % 2][:, it * 320 + k * 128:
                                          it * 320 + k * 128 + cols],
                            idn[:, :])
                    tensor.drain().then_inc(s_op, 1)
                    tensor.wait_ge(s_g2, gt_idx + 1)
                    if gt_idx == 0:
                        tensor.wait_ge(s_wl, (2 + 35) * 16)
                    for d2 in range(2):
                        for ch in range(3):
                            rows = 128 if ch < 2 else 64
                            tensor.matmul(
                                psum[4 + d2][:, :],
                                geT_sb[0:rows, ch * 128:ch * 128 + 128],
                                wo_sb[0:rows, ch * 1024 + d2 * 512:
                                      ch * 1024 + (d2 + 1) * 512],
                                start=(ch == 0), stop=(ch == 2),
                                skip_group_check=True)
                    tensor.drain().then_inc(s_op, 1)
                    for dc in range(8):
                        n2 = gt_idx * 8 + dc
                        tensor.wait_ge(s_yt, (Y2N(gt_idx, dc) + 2) * 16)
                        bank = 6 if dc < 4 else 3
                        tensor.transpose(
                            psum[bank][:, (dc % 4) * 128:(dc % 4 + 1) * 128
                                       ].bitcast(F32R),
                            ytb2[n2 % 4][:], idnr[:, :]).then_inc(s_xt, 1)
                    tensor.drain().then_inc(s_op, 1)

        @block.vector
        def _(vector):
            vector.wait_ge(s_ph, 4)
            if "lvlB" in debug_taps:
                return
            vc = [0]

            def vs(inst):
                vc[0] += 1
                inst.then_inc(s_v2, 1)
                vector.wait_ge(s_v2, vc[0])

            vector.memset(onescf[:], 1.0)
            vector.memset(zcol[:], 0.0)
            vector.memset(sc8[:], 0.125)
            vector.memset(neghalf[:], -0.5)
            vs(vector.memset(epsc[:], 1e-5))
            for b in range(2):
                for h in range(8):
                    vector.tensor_copy(v_tm[b][:, h * 66 + 64:h * 66 + 65],
                                       onescf[:])
                    vector.tensor_copy(v_tm[b][:, h * 66 + 65:h * 66 + 66],
                                       zcol[:])
            vector.nop().then_inc(s_rw1, 1)
            if not E_A:
                return
            for tc in range(NTCE):
                for dc in range(8):
                    vector.wait_ge(s_yt, (YTN(tc, dc) + 2) * 16)
                    if tc * 8 + dc - 1 > 0:
                        vector.wait_ge(s_ssm, tc * 8 + dc - 1)
                    vector.tensor_tensor(sqt[dc % 2][:],
                                         ytb[dc][:].bitcast(F32),
                                         ytb[dc][:].bitcast(F32),
                                         ALU.mult).then_inc(s_sq, 1)
                vector.wait_ge(s_st, tc * 3 + 2)
                vector.tensor_scalar(murow[:], psum[5][0:1, :],
                                     1.0 / 1024.0, None, ALU.mult)
                vs(vector.tensor_scalar(m2row[:], psum[6][0:1, :],
                                        1.0 / 1024.0, None, ALU.mult))
                vs(vector.tensor_tensor(varow[:], murow[:], murow[:],
                                        ALU.mult))
                vector.tensor_tensor(varow[:], m2row[:], varow[:],
                                     ALU.subtract).then_inc(s_rw1, 1)
                if tc >= 2:
                    vector.wait_ge(s_qk, (tc - 1) * 12)
                for dc in range(8):
                    vector.wait_ge(s_st, tc * 3 + 3)
                    vs(vector.tensor_tensor(x1n[:, 0:512],
                                            ytb[dc][:].bitcast(F32),
                                            psum[7][:, :], ALU.subtract))
                    vector.tensor_tensor(x1tb[tc % 2][dc][:], x1n[:, 0:512],
                                         psum[5][:, :],
                                         ALU.mult).then_inc(s_x1, 1)
                if not E_B:
                    continue
                for it in range(4):
                    vector.wait_ge(s_ev, EVB(tc) + 11 + 2 * it)
                    gv3 = gl_sb[:, 0:40].rearrange("p (h e) -> p h e", h=8)
                    go3 = gl_sb[:, 40:80].rearrange("p (h e) -> p h e", h=8)
                    rk4 = rkw.rearrange("p (x j h) -> p x j h", x=10, j=4)
                    for bi, g3 in ((0, gv3), (1, go3)):
                        for e in range(5):
                            jj = 0
                            for e2 in range(5):
                                if e2 == e:
                                    continue
                                op = ALU.is_ge if e2 < e else ALU.is_gt
                                mm = vector.tensor_tensor(
                                    rk4[:, bi * 5 + e, jj, :],
                                    g3[:, :, e2], g3[:, :, e], op)
                                jj += 1
                    vs(mm)
                    rka = vt40.ap()[:, 0:80]
                    rkb = vt40.ap()[:, 80:160]
                    ra4 = rka.rearrange("p (x h) -> p x h", x=10)
                    rb4 = rkb.rearrange("p (x h) -> p x h", x=10)
                    vector.tensor_tensor(ra4[:, :, :], rk4[:, :, 0, :],
                                         rk4[:, :, 1, :], ALU.add)
                    vs(vector.tensor_tensor(rb4[:, :, :], rk4[:, :, 2, :],
                                            rk4[:, :, 3, :], ALU.add))
                    vs(vector.tensor_tensor(rk_sb[:, 0:80], rka,
                                            rkb, ALU.add))
                    rvv = rk_sb[:, 0:40].rearrange("p (e h) -> p h e", h=8)
                    rvo = rk_sb[:, 40:80].rearrange("p (e h) -> p h e", h=8)
                    vector.tensor_scalar(
                        wv_sb[:].rearrange("p (h e) -> p h e", h=8),
                        rvv, 2.0, None, ALU.is_lt)
                    gt_i = tc * 4 + it
                    vs(vector.tensor_scalar(
                        msk_sb[:, gt_i * 40:(gt_i + 1) * 40].rearrange(
                            "p (h e) -> p h e", h=8),
                        rvo, 2.0, None, ALU.is_lt))
                    vs(vector.tensor_scalar(sig2[:], sig_t[:], 1.0, None,
                                            ALU.add))
                    vs(vector.reciprocal(sig3[:], sig2[:]))
                    vs(vector.tensor_tensor(wv_sb[:], wv_sb[:], sig3[:],
                                            ALU.mult))
                    p4 = vt40.ap().rearrange("p (h e c) -> p h e c", h=8, e=5)
                    for h in range(8):
                        for e in range(5):
                            mm = vector.tensor_scalar(
                                p4[:, h, e, :], xv_sb[:, e * 64:(e + 1) * 64],
                                wv_sb[:, h * 5 + e:h * 5 + e + 1], None,
                                ALU.mult)
                    vs(mm)
                    vla = vt80.ap()[:, 0:512]
                    vlb = vt80.ap()[:, 512:1024]
                    la2 = vla.rearrange("p (h c) -> p h c", h=8)
                    lb2 = vlb.rearrange("p (h c) -> p h c", h=8)
                    vector.tensor_tensor(la2[:, :, :], p4[:, :, 0, :],
                                         p4[:, :, 1, :], ALU.add)
                    vs(vector.tensor_tensor(lb2[:, :, :], p4[:, :, 2, :],
                                            p4[:, :, 3, :], ALU.add))
                    vs(vector.tensor_tensor(vla[:], vla[:], vlb[:], ALU.add))
                    vtm3 = v_tm[it % 2].ap().rearrange(
                        "p (h x) -> p h x", h=8)
                    if it >= 2:
                        vector.wait_ge(s_kw, KWB(tc) + 64 + (it - 1) * 16)
                    elif tc >= 1:
                        vector.wait_ge(s_kw, KWB(tc - 1) + 64 + (it + 3) * 16)
                    vector.tensor_tensor(vtm3[:, :, 0:64], la2[:, :, :],
                                         p4[:, :, 4, :],
                                         ALU.add).then_inc(s_gt, 1)

            if not E_C:
                return
            for ib in range(8):
                for it in range(4):
                    gt_idx = ib * 4 + it
                    if it == 0:
                        vector.wait_ge(s_at, (ib + 1) * 8)
                        if ib >= 2:
                            vector.wait_ge(s_op, OPN((ib - 2) * 4 + 3, 1))
                    a3 = att_sb[ib % 2].ap().rearrange(
                        "p (h x) -> p h x", h=8)
                    vs(vector.reciprocal(zr8[:], a3[:, :, it * 66 + 64]))
                    m3 = msk_sb[:, gt_idx * 40:(gt_idx + 1) * 40].rearrange(
                        "p (h e) -> p h e", h=8)
                    vs(vector.tensor_tensor(
                        mt_sb[:].rearrange("p (h e) -> p h e", h=8),
                        m3, zr8[:].unsqueeze(2).to_broadcast((128, 8, 5)),
                        ALU.mult))
                    q4 = vt40.ap().rearrange("p (h e c) -> p h e c", h=8, e=5)
                    for h in range(8):
                        for e in range(5):
                            mm = vector.tensor_scalar(
                                q4[:, h, e, :],
                                att_sb[ib % 2][:, h * 264 + it * 66:
                                               h * 264 + it * 66 + 64],
                                mt_sb[:, h * 5 + e:h * 5 + e + 1], None,
                                ALU.mult)
                    vs(mm)
                    w8 = vt80.ap().rearrange("p (h e c) -> p h e c", h=4, e=5)
                    vs(vector.tensor_tensor(w8[:, :, :, :], q4[:, 0:4, :, :],
                                            q4[:, 4:8, :, :], ALU.add))
                    w6 = vt40.ap()[:, 0:640].rearrange(
                        "p (h e c) -> p h e c", h=2, e=5)
                    vs(vector.tensor_tensor(w6[:, :, :, :], w8[:, 0:2, :, :],
                                            w8[:, 2:4, :, :], ALU.add))
                    vector.tensor_tensor(
                        ge_sb[ib % 2][:, it * 320:(it + 1) * 320],
                        w6[:, 0, :, :].rearrange("p e c -> p (e c)"),
                        w6[:, 1, :, :].rearrange("p e c -> p (e c)"),
                        ALU.add).then_inc(s_ge, 1)
                    vector.wait_ge(s_op, OPN(gt_idx, 3))
                    vector.tensor_copy(osb[:, 0:512], psum[4][:, :])
                    vs(vector.tensor_copy(osb[:, 512:1024], psum[5][:, :]))
                    for h2 in range(2):
                        bank = 6 if h2 == 0 else 3
                        mu_c = mrcol_sb[:, gt_idx * 2:gt_idx * 2 + 1]
                        ro_c = mrcol_sb[:, gt_idx * 2 + 1:gt_idx * 2 + 2]
                        hs = slice(h2 * 512, (h2 + 1) * 512)
                        vs(vector.tensor_copy(x1n[:, hs],
                                              psum[bank][:, :].bitcast(F32)))
                        vs(vector.tensor_scalar(x1n[:, hs], x1n[:, hs], mu_c,
                                                None, ALU.subtract))
                        vs(vector.tensor_scalar(x1n[:, hs], x1n[:, hs], ro_c,
                                                None, ALU.mult))
                    vs(vector.tensor_tensor(x1n[:], x1n[:], osb[:], ALU.add))
                    vector.reduce_sum(ln2row[:, 0:1], x1n[:],
                                      axis=mybir.AxisListType.X)
                    vs(vector.tensor_tensor(osb[:], x1n[:], x1n[:], ALU.mult))
                    vs(vector.reduce_sum(ln2row[:, 1:2], osb[:],
                                         axis=mybir.AxisListType.X))
                    vector.tensor_scalar(ln2row[:, 2:3], ln2row[:, 0:1],
                                         1.0 / 1024.0, None, ALU.mult)
                    vs(vector.tensor_scalar(ln2row[:, 3:4], ln2row[:, 1:2],
                                            1.0 / 1024.0, None, ALU.mult))
                    vs(vector.tensor_tensor(ln2row[:, 4:5], ln2row[:, 2:3],
                                            ln2row[:, 2:3], ALU.mult))
                    vector.tensor_tensor(ln2row[:, 5:6], ln2row[:, 3:4],
                                         ln2row[:, 4:5],
                                         ALU.subtract).then_inc(s_rw1, 1)
                    vector.wait_ge(s_rw2, 8 + gt_idx + 1)
                    if gt_idx >= 4:
                        vector.wait_ge(s_od, (gt_idx - 3) * 16)
                    ob = out_sb[gt_idx % 4]
                    vs(vector.tensor_scalar(ob[:], x1n[:],
                                            ln2row[:, 2:3], None,
                                            ALU.subtract))
                    vector.tensor_scalar(ob[:], ob[:],
                                         ln2row[:, 6:7], None,
                                         ALU.mult).then_inc(s_o2, 1)

        @block.scalar
        def _(scalar):
            scalar.wait_ge(s_ph, 4)
            if not E_A:
                return
            for tc in range(NTCE):
                scalar.wait_ge(s_rw1, tc + 2)
                scalar.activation(varow[:], varow[:], AF.Ln,
                                  bias=epsc[0:1, 0:1])
                scalar.activation(rorow[:], varow[:], AF.Exp,
                                  scale=neghalf[0:1, 0:1]).then_inc(s_rw2, 1)
                scalar.wait_ge(s_st, tc * 3 + 3)
                scalar.copy(mrcol_sb[:, tc * 8:tc * 8 + 8],
                            psum[6][:, 0:8]).then_inc(s_ev, 1)
                if not E_B:
                    continue
                for m in range(8):
                    qk, cc = m // 4, m % 4
                    scalar.wait_ge(s_qk, QKC(tc, m))
                    if qk == 0:
                        scalar.copy(Qt[cc][:, tc * 512:(tc + 1) * 512],
                                    psum[m % 4][:, :]).then_inc(s_ev, 1)
                    else:
                        if cc >= 2:
                            scalar.wait_ge(s_kw, KWB(tc) + (cc - 1) * 16)
                        elif tc >= 1:
                            scalar.wait_ge(s_kw, KWB(tc - 1) + (cc + 3) * 16)
                        scalar.copy(kstage[cc % 2][:],
                                    psum[m % 4][:, :]).then_inc(s_ev, 1)
                for it in range(4):
                    scalar.wait_ge(s_qk, QKC(tc, 8 + it))
                    scalar.copy(xv_sb[:], psum[4][:, 0:320])
                    scalar.copy(gl_sb[:], psum[4][:, 320:400]
                                ).then_inc(s_ev, 1)
                    scalar.activation(sig_t[:], gl_sb[:, 0:40], AF.Exp,
                                      scale=negone[:, 0:1]
                                      ).then_inc(s_ev, 1)
            if not E_C:
                return
            scalar.wait_ge(s_bar, 2)
            for ib in range(8):
                for h in range(8):
                    ibh = ib * 8 + h
                    for jb in range(64):
                        scalar.wait_ge(s_sc, JBC(ib, h, jb))
                        if JBC(ib, h, jb) - 2 > 0:
                            scalar.wait_ge(s_pv, JBC(ib, h, jb) - 2)
                        scalar.activation(exp_sb[jb % 2][:],
                                          psum[jb % 2][:, :], AF.Exp,
                                          scale=sc8[:, 0:1]
                                          ).then_inc(s_ex, 1)
                    scalar.wait_ge(s_pv, JBC(ib, h, 63))
                    if h == 0 and ib >= 2:
                        scalar.wait_ge(s_ge, (ib - 1) * 4)
                    scalar.copy(att_sb[ib % 2][:, h * 264:(h + 1) * 264],
                                psum[2][:, 0:264]).then_inc(s_at, 1)
                for it in range(4):
                    gt_idx = ib * 4 + it
                    scalar.wait_ge(s_op, OPN(gt_idx, 1))
                    scalar.activation(geT_sb[:], psum[3][:, 0:384],
                                      AF.Copy).then_inc(s_g2, 1)
                    scalar.wait_ge(s_rw1, 9 + gt_idx + 1)
                    scalar.activation(ln2row[:, 5:6], ln2row[:, 5:6], AF.Ln,
                                      bias=epsc[:, 0:1])
                    scalar.activation(ln2row[:, 6:7], ln2row[:, 5:6], AF.Exp,
                                      scale=neghalf[:, 0:1]
                                      ).then_inc(s_rw2, 1)
        at.close()

    return nc


def _barrier(nc, gpsimd, flg, mgs, fls, s_poll, slot, nopoll=False):
    nd = gpsimd.alloc_register(f"bar_nd{slot}")
    cnt = gpsimd.alloc_register(f"bar_cnt{slot}")
    flr = gpsimd.alloc_register(f"bar_flr{slot}")
    mgr = gpsimd.alloc_register(f"bar_mgr{slot}")
    gpsimd.load(mgr, mgs[0:1, slot:slot + 1])
    for arm in nc.switch_core_id(engines=[gpsimd], n=128):
        q = arm.logical & 1
        p = 1 - q
        gpsimd.dma_start(out=flg[q:q + 1, slot:slot + 1],
                         in_=mgs[0:1, slot:slot + 1]).then_inc(s_poll, 16)
        gpsimd.reg_mov(cnt, 16)
        gpsimd.reg_mov(nd, 1)
        if nopoll:
            gpsimd.dma_start(out=fls[0:1, slot:slot + 1],
                             in_=flg[p:p + 1, slot:slot + 1]).then_inc(s_poll, 16)
            gpsimd.wait_ge(s_poll, 32)
        else:
            with gpsimd.While(nd):
                gpsimd.dma_start(out=fls[0:1, slot:slot + 1],
                                 in_=flg[p:p + 1, slot:slot + 1]).then_inc(s_poll, 16)
                gpsimd.reg_add(cnt, cnt, 16)
                gpsimd.wait_ge(s_poll, cnt)
                gpsimd.load(flr, fls[0:1, slot:slot + 1])
                gpsimd.reg_alu(nd, flr, mgr, ALU.not_equal)


def _replace_last(nc, inst, new):
    for func in nc.m.functions:
        for b in func.blocks:
            il = b.instructions
            if il and (il[-1] is inst.ins or il[-1].name == inst.ins.name):
                il[-1] = new
                inst.ins = new
                return inst
    raise RuntimeError("instruction replacement failed")


def tt_acc(nc, eng, out, a, b):
    inst = eng.scalar_tensor_tensor(out, a, 1.0, b, ALU.mult, ALU.mult)
    old = inst.ins
    new = mybir.InstTensorScalarPtr(
        op0=old.op0, reverse0=old.reverse0, op1=old.op1, reverse1=old.reverse1,
        apply_transpose=old.apply_transpose,
        is_tensor_scalar_addr=old.is_tensor_scalar_addr,
        is_scalar_tensor_tensor=old.is_scalar_tensor_tensor,
        is_tensor_tensor_scan=old.is_tensor_tensor_scan,
        acc="AddAccumulate", negate_second_output=old.negate_second_output,
        name=old.name, opcode=old.opcode, engine=old.engine, debug=old.debug,
        ins=old.ins, outs=old.outs, sync_info=old.sync_info,
        emission_index=old.emission_index,
    )
    return _replace_last(nc, inst, new)


def make_inputs(inputs, seed=None):
    u = np.asarray(inputs["L0_attn_u"], dtype=np.float32)
    kk = np.asarray(inputs["L0_attn_k"], dtype=np.float32)
    skip = np.asarray(inputs["L0_attn_skip"], dtype=np.float32)
    wq = np.asarray(inputs.get("L0_ffn_Wq", np.zeros((D, 512))), np.float32)
    wk = np.asarray(inputs.get("L0_ffn_Wk", np.zeros((D, 512))), np.float32)
    ws = np.asarray(inputs.get("L0_ffn_Ws", np.zeros((D, 40))), np.float32)
    wd = np.asarray(inputs.get("L0_ffn_Wd", np.zeros((D, 40))), np.float32)
    wv5 = np.asarray(inputs.get("L0_ffn_Wv", np.zeros((5, D, 64))), np.float32)
    wo5 = np.asarray(inputs.get("L0_ffn_Wo", np.zeros((5, 64, D))), np.float32)
    wsd = np.ascontiguousarray(np.concatenate([ws, wd], axis=1))
    wv = np.ascontiguousarray(np.transpose(wv5, (1, 0, 2)).reshape(D, 320))
    wo = np.ascontiguousarray(wo5.reshape(320, D))
    rng = np.random.default_rng(seed)
    magic = rng.integers(1 << 20, 1 << 30, size=(1, 32), dtype=np.int32)
    in_maps = []
    for c in range(8):
        b, half = c >> 1, c & 1
        in_maps.append({
            "u": np.ascontiguousarray(u[b, :, half * 512:(half + 1) * 512]),
            "kk": np.ascontiguousarray(kk[half * 512:(half + 1) * 512, :]),
            "skip": np.ascontiguousarray(skip[None, half * 512:(half + 1) * 512]),
            "f1": F1_C, "tw": TW_C, "f2": F2_C, "fi": FI_C, "g": G_C,
            "mg": magic,
            "wq": wq, "wk": wk, "wsd": wsd, "wv": wv, "wo": wo,
            "iden": np.eye(128, dtype=np.float32),
            "onesd": np.ones((128, 1), dtype=np.float32),
        })
    return in_maps


_NC_CACHE = {}


def kernel(**inputs):
    key = ()
    if key not in _NC_CACHE:
        _NC_CACHE[key] = build_program()
    nc = _NC_CACHE[key]
    in_maps = make_inputs(inputs)
    res = run_bass_kernel_spmd(nc, in_maps, core_ids=list(range(8)))
    full = np.empty((B, L, D), dtype=np.float32)
    na = np.arange(64)
    for c in range(8):
        b, lam = c >> 1, c & 1
        o = np.asarray(res.results[c]["out"])
        o = o.reshape(64, 64, D)
        t_idx = (lam * 64 + np.arange(64)[:, None]) + 128 * na[None, :]
        full[b, t_idx.reshape(-1), :] = o.reshape(-1, D)
    return full
